# version 35
# baseline (speedup 1.0000x reference)
import numpy as np
import concourse.bass as bass
import concourse.mybir as mybir
from concourse.bass_utils import run_bass_kernel_spmd
from contextlib import ExitStack


F32 = mybir.dt.float32
BF16 = mybir.dt.bfloat16
AF = mybir.ActivationFunctionType
ALU = mybir.AluOpType
AX = mybir.AxisListType

D = 4096
NK = 32
NH = 8
HD = 128
NCORE = 8
EPS_LN = 1e-5
EPS_DEN = 1e-6

ENGS = ("sync", "gpsimd", "scalar", "vector", "tensor")
SELF_SEM = {"vector": "vS", "scalar": "sS", "gpsimd": "gC"}
STREAMS = ("Wq", "Wo", "Wb", "At", "Bt")
SEMS = (("qC", "qY0", "qY1", "qY2", "qY3", "qX0", "qX1", "qX2", "qX3", "qSt0", "qSt1", "qSt2", "qSt3", "qF0", "qF1", "qF2", "qF3", "gS", "gC", "tS", "vS", "sS")
        + tuple(f"{s}{i}" for s in STREAMS for i in range(3))
        + tuple(f"V{s}{i}" for s in STREAMS for i in range(3)))


class Plan:
    def __init__(self):
        self.ops = {e: [] for e in ENGS}
        self.cnt = {}
        self.lastw = {e: {} for e in ENGS}

    def c(self, sem):
        return self.cnt.get(sem, 0)

    def emit(self, eng, fn, wait=(), inc=None, inc_n=1):
        wait = list(wait)
        own = SELF_SEM.get(eng)
        if own is not None:
            wait.append((own, self.c(own)))
            if inc is None:
                inc = own
        w = []
        for s, v in wait:
            if v <= 0 or self.lastw[eng].get(s, -1) >= v:
                continue
            self.lastw[eng][s] = v
            w.append((s, v))
        if inc is not None:
            self.cnt[inc] = self.c(inc) + inc_n
        self.ops[eng].append((fn, w, inc, inc_n))
        return self.c(inc) if inc is not None else None


def build(T, TT):
    nT = T // TT
    assert T % TT == 0 and 128 <= TT <= 512 and TT % 128 == 0
    nc = bass.Bass("TRN2", target_bir_lowering=False)

    xT = nc.dram_tensor("xT", [D, T], F32, kind="ExternalInput").ap()
    wqT = nc.dram_tensor("wqT", [D, NH * HD], F32, kind="ExternalInput").ap()
    bq8 = nc.dram_tensor("bq8", [128, NH], F32, kind="ExternalInput").ap()
    woT = nc.dram_tensor("woT", [D, D], F32, kind="ExternalInput").ap()
    wbT = nc.dram_tensor("wbT", [D, D], F32, kind="ExternalInput").ap()
    bb32 = nc.dram_tensor("bb32", [128, NK], F32, kind="ExternalInput").ap()
    wrT = nc.dram_tensor("wrT", [D, 8], F32, kind="ExternalInput").ap()
    aT = nc.dram_tensor("aT", [128, D], F32, kind="ExternalInput").ap()
    bT = nc.dram_tensor("bT", [128, D], F32, kind="ExternalInput").ap()
    m3 = nc.dram_tensor("m3", [NH, 128, 128], F32, kind="ExternalInput").ap()
    z8 = nc.dram_tensor("z8", [128, NH], F32, kind="ExternalInput").ap()
    onesb = nc.dram_tensor("onesb", [128, 128], BF16, kind="ExternalInput").ap()
    ident = nc.dram_tensor("ident", [128, 128], F32, kind="ExternalInput").ap()
    outT = nc.dram_tensor("outT", [D, T], F32, kind="ExternalOutput").ap()
    sWq = nc.dram_tensor("sWq", [NK, 128, 2048], BF16, kind="Internal").ap()
    sWo = nc.dram_tensor("sWo", [4 * NH, 128, 2048], BF16, kind="Internal").ap()
    sWb = nc.dram_tensor("sWb", [4 * NK, 128, 2048], BF16, kind="Internal").ap()
    sAT = nc.dram_tensor("sAT", [4, 128, 2048], BF16, kind="Internal").ap()
    sBT = nc.dram_tensor("sBT", [4, 128, 2048], BF16, kind="Internal").ap()

    ctx = ExitStack()
    E = ctx.enter_context

    xs = E(nc.sbuf_tensor("xs", [128, NK * TT], F32))
    xnh = E(nc.sbuf_tensor("xnh", [128, NK * TT], BF16))
    xnl = E(nc.sbuf_tensor("xnl", [128, NK * TT], BF16))
    mrow = E(nc.sbuf_tensor("mrow", [128, TT], F32))
    rrow = E(nc.sbuf_tensor("rrow", [128, TT], F32))
    t32a = E(nc.sbuf_tensor("t32aa", [128, TT], F32))
    t32b = E(nc.sbuf_tensor("t32b", [128, TT], F32))
    t32c = E(nc.sbuf_tensor("t32c", [128, TT], F32))
    h32 = E(nc.sbuf_tensor("h32", [128, TT], F32))
    hsh = E(nc.sbuf_tensor("hsh", [128, TT], BF16))
    hsl = E(nc.sbuf_tensor("hsl", [128, TT], BF16))
    xch = [E(nc.sbuf_tensor(f"xch{i}", [128, TT], BF16)) for i in range(2)]
    xcl = [E(nc.sbuf_tensor(f"xcl{i}", [128, TT], BF16)) for i in range(2)]
    xhib = xch
    sqb = xcl
    NSLOT = {"Wq": 2, "Wo": 2, "Wb": 3, "At": 1, "Bt": 1}
    wbuf = {s: [E(nc.sbuf_tensor(f"{s}hl{i}", [128, 2048], BF16)) for i in range(NSLOT[s])]
            for s in STREAMS}
    wst = [E(nc.sbuf_tensor(f"wst{i}", [128, 1024], F32)) for i in range(4)]
    m3s = E(nc.sbuf_tensor("m3s", [128, NH * 128], F32))
    z8s = E(nc.sbuf_tensor("z8s", [128, NH], F32))
    zrep = E(nc.sbuf_tensor("zrep", [128, 128], F32))
    wrh = E(nc.sbuf_tensor("wrh", [128, NK * 8], BF16))
    wrl = E(nc.sbuf_tensor("wrl", [128, NK * 8], BF16))
    bq8s = E(nc.sbuf_tensor("bq8s", [128, NH], F32))
    bb32s = E(nc.sbuf_tensor("bb32s", [128, NK], F32))
    onesbs = E(nc.sbuf_tensor("onesbs", [128, 128], BF16))
    idents = E(nc.sbuf_tensor("idents", [128, 128], F32))
    epsc = E(nc.sbuf_tensor("epsc", [128, 1], F32))
    epsd = E(nc.sbuf_tensor("epsd", [128, 1], F32))
    lfm = E(nc.sbuf_tensor("lfm", [8, TT], F32))
    wtok = E(nc.sbuf_tensor("wtok", [128, 32], F32))
    wrep = E(nc.sbuf_tensor("wrep", [128, 128], F32))
    rcol = [E(nc.sbuf_tensor(f"rcol{i}", [128, 1], F32)) for i in range(4)]
    pb = [E(nc.psum_tensor(f"pb{i}", [128, 512], F32)) for i in range(8)]

    sem = {n: E(nc.semaphore(n)) for n in SEMS}
    block = E(nc.Block())

    P = Plan()
    sig32 = xnh[:].bitcast(F32)

    def dma(eng, q, out, in_, wait=(), accum=False):
        def f(e, out=out, in_=in_, accum=accum):
            if accum:
                return e.dma_start(out=out, in_=in_, accum_op=ALU.add)
            return e.dma_start(out=out, in_=in_)
        return P.emit(eng, f, wait=wait, inc=q, inc_n=16)

    dma("sync", "qC", bq8s[:], bq8)
    dma("sync", "qC", bb32s[:], bb32)
    dma("sync", "qC", onesbs[:], onesb)
    dma("sync", "qC", idents[:], ident)
    dma("sync", "qC", z8s[:], z8)
    wr32 = wst[0]
    for k in range(NK):
        dma("sync", "qC", wr32[:, k * 8:(k + 1) * 8], wrT[k * 128:(k + 1) * 128, :])
    for h in range(NH):
        dma("sync", "qC", m3s[:, h * 128:(h + 1) * 128], m3[h])
    CR = P.c("qC")
    P.emit("vector", lambda e: e.memset(epsc[:], EPS_LN))
    epsd_v = P.emit("vector", lambda e: e.memset(epsd[:], EPS_DEN))
    wr_s = P.emit("scalar", lambda e: e.activation(out=wrh[:], in_=wr32[:, :NK * 8], func=AF.Copy, bias=0.0, scale=1.0),
                  wait=[("qC", CR)])
    wr_v = P.emit("vector", lambda e: e.tensor_sub(wrl[:], wr32[:, :NK * 8], wrh[:]),
                  wait=[("qC", CR), ("sS", wr_s)])
    st_free = [("vS", wr_v), ("vS", 0), ("vS", 0), ("vS", 0)]
    p0i = [0]

    class WS:

        def __init__(self, name, width, srcs_fn, scr):
            self.name, self.w = name, width
            self.ns = NSLOT[name]
            self.srcs_fn, self.scr = srcs_fn, scr
            self.free = [("tS", 0)] * self.ns
            self.milestone = None

        def get(self, idx, tile):
            i = idx % self.ns
            hb = wbuf[self.name][i][:, :self.w]
            lb = wbuf[self.name][i][:, self.w:2 * self.w]
            if tile == 0:
                srcs = self.srcs_fn(idx)
                if len(srcs) == 1:
                    sti = p0i[0] % 4
                    p0i[0] += 1
                    stg = wst[sti][:, :self.w]
                    stq = f"qSt{sti}"
                    dma("sync", stq, stg, srcs[0], wait=[st_free[sti]])
                    wq = [(stq, P.c(stq))]
                else:
                    tj = 2 * (p0i[0] % 2)
                    p0i[0] += 1
                    stg = wst[tj][:, :self.w]
                    stg1 = wst[tj + 1][:, :self.w]
                    dma("sync", f"qSt{tj}", stg, srcs[0], wait=[st_free[tj]])
                    pa = P.c(f"qSt{tj}")
                    dma("sync", f"qSt{tj + 1}", stg1, srcs[2], wait=[st_free[tj + 1]])
                    pbv = P.c(f"qSt{tj + 1}")
                    dma("gpsimd", f"qF{tj}", stg, srcs[1], wait=[(f"qSt{tj}", pa)], accum=True)
                    aa = P.c(f"qF{tj}")
                    dma("gpsimd", f"qF{tj + 1}", stg1, srcs[3], wait=[(f"qSt{tj + 1}", pbv)], accum=True)
                    ab = P.c(f"qF{tj + 1}")
                    vm_ = P.emit("vector", lambda e, o=stg, i1=stg1: e.tensor_add(o, o, i1),
                                 wait=[(f"qF{tj}", aa), (f"qF{tj + 1}", ab)])
                    st_free[tj + 1] = ("vS", vm_)
                    wq = [("vS", vm_)]
                vq = f"V{self.name}{i}"
                sh = P.emit("scalar", lambda e, o=hb, ii=stg: e.activation(
                    out=o, in_=ii, func=AF.Copy, bias=0.0, scale=1.0),
                    wait=wq + [self.free[i], (vq, P.c(vq))])
                sl = P.emit("vector", lambda e, o=lb, ii=stg, hh=hb: e.tensor_sub(o, ii, hh),
                            wait=wq + [("sS", sh), self.free[i], (vq, P.c(vq))])
                st_free[tj if len(srcs) > 1 else sti] = ("vS", sl)
                dma("scalar", vq, self.scr(idx), wbuf[self.name][i][:, :2 * self.w],
                    wait=[("sS", sh), ("vS", sl)])
                return hb, lb, [("sS", sh), ("vS", sl)]
            sq = f"{self.name}{i}"
            dma("sync", sq, wbuf[self.name][i][:, :2 * self.w], self.scr(idx),
                wait=[self.free[i], ("gS", self.milestone)])
            return hb, lb, [(sq, P.c(sq))]

        def done(self, idx, tS_val):
            self.free[idx % self.ns] = ("tS", tS_val)

        def finish_tile0(self):
            self.milestone = P.emit(
                "gpsimd", lambda e: e.nop(),
                wait=[(f"V{self.name}{i}", P.c(f"V{self.name}{i}")) for i in range(self.ns)],
                inc="gS")

    ws_q = WS("Wq", NH * HD, lambda k: [wqT[k * 128:(k + 1) * 128, :]],
              lambda k: sWq[k])
    ws_o = WS("Wo", 1024,
              lambda idx: [woT[(4 * (idx % NH) + g) * 128:(4 * (idx % NH) + g + 1) * 128,
                               (idx // NH) * 1024:(idx // NH + 1) * 1024] for g in range(4)],
              lambda idx: sWo[(idx // NH) * NH + idx % NH])
    ws_b = WS("Wb", 1024,
              lambda idx: [wbT[(idx % NK) * 128:(idx % NK + 1) * 128,
                               ((idx // NK) % 4) * 1024:((idx // NK) % 4 + 1) * 1024]],
              lambda idx: sWb[((idx // NK) % 4) * NK + idx % NK])
    ws_a = WS("At", 1024, lambda idx: [aT[:, idx * 1024:(idx + 1) * 1024]],
              lambda idx: sAT[idx])
    ws_t = WS("Bt", 1024, lambda idx: [bT[:, idx * 1024:(idx + 1) * 1024]],
              lambda idx: sBT[idx])

    st_done = [("qY0", 0), ("qY1", 0), ("qY2", 0), ("qY3", 0)]
    xhib_free = [("tS", 0)] * 2
    sqb_free = [("tS", 0)] * 2
    pb_free = [("vS", 0)] * 8
    ng = TT // 128

    def emit_xloads(tt, groups):
        out = []
        for g in groups:
            dma("scalar", f"qX{g}", xs[:, g * 8 * TT:(g + 1) * 8 * TT],
                bass.AP(xT.tensor, xT.offset + (g * 1024) * T + tt * TT,
                        [[T, 128], [128 * T, 8], [1, TT]]), wait=[st_done[g]])
            out.append((f"qX{g}", P.c(f"qX{g}")))
        return out

    next_loads = emit_xloads(0, range(4))
    for t in range(nT):
        tok0 = t * TT
        xload = [next_loads[k // 8] for k in range(NK)]
        next_loads_acc = []

        def ln_stats(gate):
            nonlocal xhib_free, sqb_free
            for k in range(NK):
                i = k % 2
                if k % 2 == 0:
                    vh = P.emit("vector", lambda e, k=k, i=i: e.tensor_copy(
                        xhib[i][:], xs[:, k * TT:(k + 1) * TT]),
                        wait=gate(k) + [xhib_free[i]])
                    hsem = "vS"
                    vq = P.emit("scalar", lambda e, i=i: e.activation(
                        out=sqb[i][:], in_=xhib[i][:], func=AF.Square, bias=0.0, scale=1.0),
                        wait=[sqb_free[i], ("vS", vh)])
                    qsem = "sS"
                else:
                    vh = P.emit("scalar", lambda e, k=k, i=i: e.activation(
                        out=xhib[i][:], in_=xs[:, k * TT:(k + 1) * TT], func=AF.Copy, bias=0.0, scale=1.0),
                        wait=gate(k) + [xhib_free[i]])
                    hsem = "sS"
                    vq = P.emit("vector", lambda e, i=i: e.tensor_mul(
                        sqb[i][:], xhib[i][:], xhib[i][:]),
                        wait=[sqb_free[i], ("sS", vh)])
                    qsem = "vS"
                tv = P.emit("tensor", lambda e, k=k, i=i: e.matmul(
                    pb[0][:, :TT], onesbs[:], xhib[i][:], start=(k == 0), stop=(k == NK - 1)),
                    wait=[(hsem, vh), ("qC", CR)] + ([pb_free[0]] if k == 0 else []), inc="tS")
                xhib_free[i] = ("tS", tv)
                tv2 = P.emit("tensor", lambda e, k=k, i=i: e.matmul(
                    pb[1][:, :TT], onesbs[:], sqb[i][:], start=(k == 0), stop=(k == NK - 1)),
                    wait=[(qsem, vq)] + ([pb_free[1]] if k == 0 else []), inc="tS")
                sqb_free[i] = ("tS", tv2)
            tdone = P.c("tS")
            v1 = P.emit("vector", lambda e: e.tensor_scalar(mrow[:], pb[0][:, :TT], 1.0 / D, None, ALU.mult),
                        wait=[("tS", tdone)])
            v2 = P.emit("vector", lambda e: e.tensor_mul(t32a[:], mrow[:], mrow[:]))
            v3 = P.emit("vector", lambda e: e.scalar_tensor_tensor(
                t32b[:], pb[1][:, :TT], 1.0 / D, t32a[:], ALU.mult, ALU.subtract))
            s1 = P.emit("scalar", lambda e: e.activation(out=t32c[:], in_=t32b[:], func=AF.Ln,
                                                         bias=epsc[:], scale=1.0), wait=[("vS", v3)])
            s2 = P.emit("scalar", lambda e: e.activation(out=rrow[:], in_=t32c[:], func=AF.Exp,
                                                         bias=0.0, scale=-0.5))
            pb_free[0] = ("vS", v1)
            pb_free[1] = ("vS", v3)
            return v1, ("sS", s2)

        m_r, r_r = ln_stats(lambda k: [xload[k]])

        prev_cast = 0
        for k in range(NK):
            i = k % 2
            va = P.emit("vector", lambda e, k=k: e.tensor_sub(t32a[:], xs[:, k * TT:(k + 1) * TT], mrow[:]),
                        wait=[("vS", m_r)])
            vsc = P.emit("vector", lambda e: e.tensor_mul(t32b[:], t32a[:], rrow[:]),
                         wait=[r_r, ("sS", prev_cast)])
            sh = P.emit("scalar", lambda e, i=i: e.activation(out=xch[i][:], in_=t32b[:], func=AF.Copy,
                                                              bias=0.0, scale=1.0),
                        wait=[("vS", vsc), xhib_free[i]])
            prev_cast = sh
            vl = P.emit("vector", lambda e, i=i: e.tensor_sub(xcl[i][:], t32b[:], xch[i][:]),
                        wait=[("sS", sh), sqb_free[i]])
            whb, wlb, wready = ws_q.get(k, t)
            tv = None
            for o in range(NH):
                P.emit("tensor", lambda e, w=whb, x=xch[i], o=o, st=(k == 0): e.matmul(
                    pb[o][:, :TT], w[:, o * 128:(o + 1) * 128], x[:], start=st, stop=False),
                    wait=[("sS", sh)] + wready + ([pb_free[o]] if k == 0 else []))
                P.emit("tensor", lambda e, w=whb, x=xcl[i], o=o: e.matmul(
                    pb[o][:, :TT], w[:, o * 128:(o + 1) * 128], x[:], start=False, stop=False),
                    wait=[("vS", vl)])
                tv = P.emit("tensor", lambda e, w=wlb, x=xch[i], o=o, lq=(k == NK - 1): e.matmul(
                    pb[o][:, :TT], w[:, o * 128:(o + 1) * 128], x[:], start=False, stop=lq),
                    inc="tS")
            xhib_free[i] = ("tS", tv)
            sqb_free[i] = ("tS", tv)
            ws_q.done(k, tv)
            if k == NK - 1:
                q_bank = list(range(P.c("tS") - NH + 1, P.c("tS") + 1))
        if t == 0:
            ws_q.finish_tile0()

        prev_vs = [0, 0]
        sig_v = []
        for o in range(NH):
            ea, eb = (t32c, t32b) if o % 2 == 0 else (mrow, rrow)
            se = P.emit("scalar", lambda e, o=o, ea=ea: e.activation(
                out=ea[:], in_=pb[o][:, :TT], func=AF.Exp, bias=bq8s[:, o:o + 1], scale=1.0),
                wait=[("tS", q_bank[o]), ("vS", prev_vs[o % 2]), ("qC", CR)])
            sr = P.emit("scalar", lambda e, o=o, eb=eb: e.activation(
                out=eb[:], in_=pb[o][:, :TT], func=AF.Relu, bias=bq8s[:, o:o + 1], scale=1.0))
            vs = P.emit("vector", lambda e, o=o, ea=ea, eb=eb: e.scalar_tensor_tensor(
                sig32[:, o * TT:(o + 1) * TT], ea[:], 1.0, eb[:], ALU.min, ALU.add),
                wait=[("sS", sr)])
            prev_vs[o % 2] = vs
            sig_v.append(vs)
            pb_free[o] = ("sS", sr)

        prev_td = 0
        prev_sh2 = 0
        prev_vl2 = 0
        retr_hv = []
        for h in range(NH):
            bn, bd = h % 4, 4 + h % 4
            vz = P.emit("vector", lambda e, h=h: e.tensor_copy(
                zrep[:], bass.AP(z8s[:].tensor, z8s[:].offset + h, [[NH, 128], [0, 128]])),
                wait=[("qC", CR), ("tS", prev_td)])
            tn = P.emit("tensor", lambda e, h=h, bn=bn: e.matmul(
                pb[bn][:, :TT], m3s[:, h * 128:(h + 1) * 128], sig32[:, h * TT:(h + 1) * TT],
                start=True, stop=True),
                wait=[("vS", sig_v[h]), pb_free[bn], ("qC", CR)], inc="tS")
            td = P.emit("tensor", lambda e, h=h, bd=bd: e.matmul(
                pb[bd][:, :TT], zrep[:], sig32[:, h * TT:(h + 1) * TT], start=True, stop=True),
                wait=[("vS", vz), pb_free[bd]], inc="tS")
            prev_td = td
            sd = P.emit("scalar", lambda e, bd=bd: e.activation(
                out=t32a[:], in_=pb[bd][:, :TT], func=AF.Ln, bias=epsd[:], scale=1.0),
                wait=[("tS", td), ("vS", epsd_v)])
            sr2 = P.emit("scalar", lambda e: e.activation(
                out=t32b[:], in_=t32a[:], func=AF.Exp, bias=0.0, scale=-1.0))
            vm2 = P.emit("vector", lambda e, bn=bn: e.tensor_mul(t32c[:], pb[bn][:, :TT], t32b[:]),
                         wait=[("tS", tn), ("sS", sr2), ("vS", prev_vl2)])
            sh2 = P.emit("scalar", lambda e, h=h: e.activation(
                out=xnl[:, h * TT:(h + 1) * TT], in_=t32c[:], func=AF.Copy, bias=0.0, scale=1.0),
                wait=[("vS", vm2)])
            prev_sh2 = sh2
            vl2 = P.emit("vector", lambda e, h=h: e.tensor_sub(
                xnl[:, (NH + h) * TT:(NH + h + 1) * TT], t32c[:], xnl[:, h * TT:(h + 1) * TT]),
                wait=[("sS", sh2)])
            prev_vl2 = vl2
            retr_hv.append([("sS", sh2), ("vS", vl2)])
            pb_free[bn] = ("vS", vm2)
            pb_free[bd] = ("sS", sd)

        ev_done = []
        for og in range(4):
            tv_last = [None] * 8
            for h in range(NH):
                whb, wlb, wready = ws_o.get(og * NH + h, t)
                for oi in range(8):
                    P.emit("tensor", lambda e, w=whb, h=h, oi=oi: e.matmul(
                        pb[oi][:, :TT], w[:, oi * 128:(oi + 1) * 128], xnl[:, h * TT:(h + 1) * TT],
                        start=(h == 0), stop=False),
                        wait=wready + retr_hv[h] + ([pb_free[oi]] if h == 0 else []))
                    P.emit("tensor", lambda e, w=whb, h=h, oi=oi: e.matmul(
                        pb[oi][:, :TT], w[:, oi * 128:(oi + 1) * 128],
                        xnl[:, (NH + h) * TT:(NH + h + 1) * TT], start=False, stop=False))
                    r = P.emit("tensor", lambda e, w=wlb, h=h, oi=oi: e.matmul(
                        pb[oi][:, :TT], w[:, oi * 128:(oi + 1) * 128], xnl[:, h * TT:(h + 1) * TT],
                        start=False, stop=(h == NH - 1)),
                        inc="tS" if (oi == 7 or h == NH - 1) else None)
                    if h == NH - 1:
                        tv_last[oi] = r
                ws_o.done(og * NH + h, P.c("tS"))
            for oi in range(8):
                o = og * 8 + oi
                ve = P.emit("vector", lambda e, o=o, b=oi: e.tensor_add(
                    xs[:, o * TT:(o + 1) * TT], pb[b][:, :TT], xs[:, o * TT:(o + 1) * TT]),
                    wait=[("tS", tv_last[oi])])
                ev_done.append(ve)
                pb_free[oi] = ("vS", ve)
        if t == 0:
            ws_o.finish_tile0()

        m_r, r_r = ln_stats(lambda k: [("vS", ev_done[k])])

        pre_a = ws_a.get(0, t)
        pre_t0 = ws_t.get(0, t)
        pre_b = [ws_b.get(k, t) for k in range(3)]

        bhb0, blb0, bready0 = pre_t0
        ahb = alb = None
        aready = []
        prev_gsh = 0
        xn_k = []
        for k in range(NK):
            va = P.emit("vector", lambda e, k=k: e.tensor_sub(t32a[:], xs[:, k * TT:(k + 1) * TT], mrow[:]),
                        wait=[r_r, ("sS", prev_gsh)])
            vb = P.emit("vector", lambda e: e.tensor_mul(t32b[:], t32a[:], rrow[:]))
            gsh = P.emit("scalar", lambda e, k=k: e.activation(
                out=xnh[:, k * TT:(k + 1) * TT], in_=t32b[:], func=AF.Copy, bias=0.0, scale=1.0),
                wait=[("vS", vb)])
            prev_gsh = gsh
            gvl = P.emit("vector", lambda e, k=k: e.tensor_sub(
                xnl[:, k * TT:(k + 1) * TT], t32b[:], xnh[:, k * TT:(k + 1) * TT]),
                wait=[("sS", gsh)])
            xn_k.append([("sS", gsh), ("vS", gvl)])
            if k % 8 == 0:
                ahb, alb, aready = pre_a if k == 0 else ws_a.get(k // 8, t)
            kk = k % 8
            P.emit("tensor", lambda e, k=k: e.matmul(
                pb[1][0:8, :TT], wrh[:, k * 8:(k + 1) * 8], xnh[:, k * TT:(k + 1) * TT],
                start=(k == 0), stop=False),
                wait=[("sS", wr_s), ("vS", wr_v)] + xn_k[k] + ([pb_free[1]] if k == 0 else []))
            P.emit("tensor", lambda e, k=k: e.matmul(
                pb[1][0:8, :TT], wrh[:, k * 8:(k + 1) * 8], xnl[:, k * TT:(k + 1) * TT],
                start=False, stop=False))
            P.emit("tensor", lambda e, k=k: e.matmul(
                pb[1][0:8, :TT], wrl[:, k * 8:(k + 1) * 8], xnh[:, k * TT:(k + 1) * TT],
                start=False, stop=(k == NK - 1)), inc="tS" if k == NK - 1 else None)
            P.emit("tensor", lambda e, w=ahb, k=k, kk=kk: e.matmul(
                pb[0][:, :TT], w[:, kk * 128:(kk + 1) * 128], xnh[:, k * TT:(k + 1) * TT],
                start=(k == 0), stop=False),
                wait=aready + ([pb_free[0]] if k == 0 else []))
            P.emit("tensor", lambda e, w=ahb, k=k, kk=kk: e.matmul(
                pb[0][:, :TT], w[:, kk * 128:(kk + 1) * 128], xnl[:, k * TT:(k + 1) * TT],
                start=False, stop=False))
            P.emit("tensor", lambda e, w=alb, k=k, kk=kk: e.matmul(
                pb[0][:, :TT], w[:, kk * 128:(kk + 1) * 128], xnh[:, k * TT:(k + 1) * TT],
                start=False, stop=(k == NK - 1)), inc="tS" if k % 8 == 7 else None)
            if k % 8 == 7:
                ws_a.done(k // 8, P.c("tS"))
            whb, wlb, wready = pre_b[k] if k < 3 else ws_b.get(0 * NK + k, t)
            for oi in range(6):
                b = 2 + oi
                P.emit("tensor", lambda e, w=whb, oi=oi, k=k, b=b: e.matmul(
                    pb[b][:, :TT], w[:, oi * 128:(oi + 1) * 128], xnh[:, k * TT:(k + 1) * TT],
                    start=(k == 0), stop=False),
                    wait=wready + ([pb_free[b]] if k == 0 else []))
                P.emit("tensor", lambda e, w=whb, oi=oi, k=k, b=b: e.matmul(
                    pb[b][:, :TT], w[:, oi * 128:(oi + 1) * 128], xnl[:, k * TT:(k + 1) * TT],
                    start=False, stop=False))
                P.emit("tensor", lambda e, w=wlb, oi=oi, k=k, b=b: e.matmul(
                    pb[b][:, :TT], w[:, oi * 128:(oi + 1) * 128], xnh[:, k * TT:(k + 1) * TT],
                    start=False, stop=False),
                    inc="tS" if oi == 5 else None)
            ws_b.done(0 * NK + k, P.c("tS"))
        la_done = P.c("tS")
        vlf = P.emit("vector", lambda e: e.tensor_copy(lfm[:], pb[1][0:8, :TT]), wait=[("tS", la_done)])
        vh32 = P.emit("vector", lambda e: e.tensor_copy(h32[:], pb[0][:, :TT]))
        pb_free[0] = ("vS", vh32)
        pb_free[1] = ("vS", vlf)
        ttr = []
        lt_read = {0: [("vS", vh32)], 1: [("vS", vlf)]}
        for g in range(ng):
            b = g % 2
            tg = P.emit("tensor", lambda e, g=g, b=b: e.transpose(
                pb[b][:, 0:8], lfm[0:8, g * 128:(g + 1) * 128], idents[0:8, 0:8]),
                wait=lt_read[b] + [("qC", CR)], inc="tS")
            ttr.append(tg)
            if g % 2 == 1 or g == ng - 1:
                for gg in range(g - (g % 2), g + 1):
                    bb_ = gg % 2
                    lt = pb[bb_][:, 0:8]
                    vm1 = P.emit("vector", lambda e, lt=lt: e.reduce_max(rcol[0][:], lt, axis=AX.X),
                                 wait=[("tS", ttr[gg])])
                    vn1 = P.emit("vector", lambda e: e.tensor_scalar(rcol[1][:], rcol[0][:], -1.0, None, ALU.mult))
                    veq = P.emit("vector", lambda e, lt=lt: e.tensor_scalar(
                        t32a[:, :8], lt, rcol[0][:], None, ALU.is_equal))
                    vl2 = P.emit("vector", lambda e, lt=lt: e.scalar_tensor_tensor(
                        t32b[:, :8], t32a[:, :8], -1e30, lt, ALU.mult, ALU.add))
                    vm2 = P.emit("vector", lambda e: e.reduce_max(rcol[2][:], t32b[:, :8], axis=AX.X))
                    vke = P.emit("vector", lambda e, lt=lt: e.tensor_scalar(
                        t32c[:, :8], lt, rcol[2][:], None, ALU.is_ge))
                    sex = P.emit("scalar", lambda e, lt=lt: e.activation(
                        out=t32b[:, :8], in_=lt, func=AF.Exp, bias=rcol[1][:], scale=1.0),
                        wait=[("vS", vke)])
                    vek = P.emit("vector", lambda e: e.tensor_mul(t32b[:, :8], t32b[:, :8], t32c[:, :8]),
                                 wait=[("sS", sex)])
                    vsm = P.emit("vector", lambda e: e.reduce_sum(rcol[3][:], t32b[:, :8], axis=AX.X))
                    vrc = P.emit("vector", lambda e: e.reciprocal(rcol[3][:], rcol[3][:]))
                    wt_ready = P.emit("vector", lambda e, gg=gg: e.tensor_scalar(
                        wtok[:, gg * 8:(gg + 1) * 8], t32b[:, :8], rcol[3][:], 2.0, ALU.mult, ALU.mult))
                    lt_read[bb_] = [("sS", sex), ("vS", vl2)]
        if t == 0:
            ws_a.finish_tile0()
            ws_b.milestone = P.emit(
                "gpsimd", lambda e: e.nop(),
                wait=[(f"VWb{i}", P.c(f"VWb{i}")) for i in range(ws_b.ns)], inc="gS")
        for k in range(NK):
            whb, wlb, wready = ws_b.get(4 * NK + k, max(t, 1))
            for j in range(2):
                oi = 6 + j
                P.emit("tensor", lambda e, w=whb, oi=oi, k=k, b=j: e.matmul(
                    pb[b][:, :TT], w[:, oi * 128:(oi + 1) * 128], xnh[:, k * TT:(k + 1) * TT],
                    start=(k == 0), stop=False),
                    wait=wready + (lt_read[j] if k == 0 else []))
                P.emit("tensor", lambda e, w=whb, oi=oi, k=k, b=j: e.matmul(
                    pb[b][:, :TT], w[:, oi * 128:(oi + 1) * 128], xnl[:, k * TT:(k + 1) * TT],
                    start=False, stop=False))
                P.emit("tensor", lambda e, w=wlb, oi=oi, k=k, b=j: e.matmul(
                    pb[b][:, :TT], w[:, oi * 128:(oi + 1) * 128], xnh[:, k * TT:(k + 1) * TT],
                    start=False, stop=(k == NK - 1)),
                    inc="tS" if j == 1 else None)
            ws_b.done(4 * NK + k, P.c("tS"))
        res_done = P.c("tS")
        ev6 = P.emit("vector", lambda e: e.tensor_copy(mrow[:], pb[0][:, :TT]), wait=[("tS", res_done)])
        ev7 = P.emit("vector", lambda e: e.tensor_copy(rrow[:], pb[1][:, :TT]))
        wfm_read = {0: ("vS", ev6), 1: ("vS", ev7)}
        vhs_l = []
        for g in range(ng):
            b = g % 2
            vrep = P.emit("vector", lambda e, g=g: e.tensor_copy(
                wrep[:].rearrange("p (a b) -> p a b", a=8),
                bass.AP(wtok[:].tensor, wtok[:].offset + g * 8, [[32, 128], [1, 8], [0, 16]])),
                wait=[("vS", wt_ready)])
            twf = P.emit("tensor", lambda e, g=g, b=b: e.transpose(
                pb[b][:, 0:128], wrep[:], idents[:]),
                wait=[("vS", vrep), wfm_read[b]], inc="tS")
            vhs = P.emit("vector", lambda e, g=g, b=b: e.tensor_mul(
                t32a[:, g * 128:(g + 1) * 128], h32[:, g * 128:(g + 1) * 128], pb[b][:, 0:128]),
                wait=[("tS", twf)])
            wfm_read[b] = ("vS", vhs)
            vhs_l.append(vhs)
        shh = P.emit("scalar", lambda e: e.activation(out=hsh[:], in_=t32a[:], func=AF.Copy,
                                                      bias=0.0, scale=1.0),
                     wait=[("vS", P.c("vS"))])
        vhl = P.emit("vector", lambda e: e.tensor_sub(hsl[:], t32a[:], hsh[:]), wait=[("sS", shh)])
        hs_s, hs_v = shh, vhl
        pre_nb = ws_b.get(1 * NK, t)
        for oi in range(8):
            o = oi
            b = 2 + oi if oi < 6 else oi - 6
            st_ = (oi >= 6)
            P.emit("tensor", lambda e, w=bhb0, b=b, oi=oi, st_=st_: e.matmul(
                pb[b][:, :TT], w[:, oi * 128:(oi + 1) * 128], hsh[:], start=st_, stop=False),
                wait=bready0 + [("sS", hs_s), ("vS", hs_v)] + ([wfm_read[b]] if st_ else []))
            P.emit("tensor", lambda e, w=bhb0, b=b, oi=oi: e.matmul(
                pb[b][:, :TT], w[:, oi * 128:(oi + 1) * 128], hsl[:], start=False, stop=False))
            tb = P.emit("tensor", lambda e, w=blb0, b=b, oi=oi: e.matmul(
                pb[b][:, :TT], w[:, oi * 128:(oi + 1) * 128], hsh[:], start=False, stop=True),
                inc="tS")
            if oi < 6:
                vf = P.emit("vector", lambda e, o=o, b=b: e.scalar_tensor_tensor(
                    xs[:, o * TT:(o + 1) * TT], pb[b][:, :TT], bb32s[:, o:o + 1],
                    xs[:, o * TT:(o + 1) * TT], ALU.add, ALU.add),
                    wait=[("tS", tb)])
            else:
                base_sb = mrow if oi == 6 else rrow
                tmp = t32c if oi == 6 else t32b
                vfa = P.emit("vector", lambda e, o=o, b=b, base_sb=base_sb, tmp=tmp: e.scalar_tensor_tensor(
                    tmp[:], pb[b][:, :TT], bb32s[:, o:o + 1], base_sb[:], ALU.add, ALU.add),
                    wait=[("tS", tb)])
                vf = P.emit("vector", lambda e, o=o, tmp=tmp: e.tensor_add(
                    xs[:, o * TT:(o + 1) * TT], tmp[:], xs[:, o * TT:(o + 1) * TT]))
            pb_free[b] = ("vS", vf)
        dma("scalar", "qY0",
            bass.AP(outT.tensor, outT.offset + tok0, [[T, 128], [128 * T, 8], [1, TT]]),
            xs[:, :8 * TT], wait=[("vS", vf)])
        ws_t.done(0, P.c("tS"))
        if t < nT - 1:
            st_done[0] = ("qY0", P.c("qY0"))
            next_loads_acc += emit_xloads(t + 1, [0])
        for og in range(1, 4):
            bhb, blb, bready = ws_t.get(og, t)
            for k in range(NK):
                whb, wlb, wready = pre_nb if k == 0 else ws_b.get(og * NK + k, t)
                for oi in range(8):
                    P.emit("tensor", lambda e, w=whb, oi=oi, k=k: e.matmul(
                        pb[oi][:, :TT], w[:, oi * 128:(oi + 1) * 128], xnh[:, k * TT:(k + 1) * TT],
                        start=(k == 0), stop=False),
                        wait=wready + ([pb_free[oi]] if k == 0 else []))
                    P.emit("tensor", lambda e, w=whb, oi=oi, k=k: e.matmul(
                        pb[oi][:, :TT], w[:, oi * 128:(oi + 1) * 128], xnl[:, k * TT:(k + 1) * TT],
                        start=False, stop=False))
                    P.emit("tensor", lambda e, w=wlb, oi=oi, k=k: e.matmul(
                        pb[oi][:, :TT], w[:, oi * 128:(oi + 1) * 128], xnh[:, k * TT:(k + 1) * TT],
                        start=False, stop=False),
                        inc="tS" if oi == 7 else None)
                ws_b.done(og * NK + k, P.c("tS"))
                if k == NK - 1 and og < 3:
                    pre_nb = ws_b.get((og + 1) * NK, t)
            for oi in range(8):
                o = og * 8 + oi
                P.emit("tensor", lambda e, w=bhb, b=oi, oi=oi: e.matmul(
                    pb[b][:, :TT], w[:, oi * 128:(oi + 1) * 128], hsh[:], start=False, stop=False),
                    wait=bready + [("sS", hs_s), ("vS", hs_v)])
                P.emit("tensor", lambda e, w=bhb, b=oi, oi=oi: e.matmul(
                    pb[b][:, :TT], w[:, oi * 128:(oi + 1) * 128], hsl[:], start=False, stop=False))
                tb = P.emit("tensor", lambda e, w=blb, b=oi, oi=oi: e.matmul(
                    pb[b][:, :TT], w[:, oi * 128:(oi + 1) * 128], hsh[:], start=False, stop=True),
                    inc="tS")
                vf = P.emit("vector", lambda e, o=o, b=oi: e.scalar_tensor_tensor(
                    xs[:, o * TT:(o + 1) * TT], pb[b][:, :TT], bb32s[:, o:o + 1],
                    xs[:, o * TT:(o + 1) * TT], ALU.add, ALU.add),
                    wait=[("tS", tb)])
                pb_free[oi] = ("vS", vf)
            dma("scalar", f"qY{og}",
                bass.AP(outT.tensor, outT.offset + (og * 1024) * T + tok0,
                        [[T, 128], [128 * T, 8], [1, TT]]),
            xs[:, og * 8 * TT:(og + 1) * 8 * TT], wait=[("vS", vf)])
            ws_t.done(og, P.c("tS"))
            if t < nT - 1:
                st_done[og] = (f"qY{og}", P.c(f"qY{og}"))
                next_loads_acc += emit_xloads(t + 1, [og])
        if t == 0:
            ws_b.finish_tile0()
            ws_t.finish_tile0()
        next_loads = next_loads_acc

    P.emit("scalar", lambda e: e.nop(), wait=[(f"qY{i}", P.c(f"qY{i}")) for i in range(4)] + [(f"qX{i}", P.c(f"qX{i}")) for i in range(4)], inc="sS")
    P.emit("sync", lambda e: e.nop(),
           wait=[(s, P.c(s)) for s in
                 [f"{s}{i}" for s in STREAMS for i in range(2)] + ["qSt0", "qSt1", "qSt2", "qSt3", "qC"]])
    P.emit("gpsimd", lambda e: e.nop(),
           wait=[(s, P.c(s)) for s in
                 [f"V{s}{i}" for s in STREAMS for i in range(2)] + ["qF0", "qF1", "gS"]])
    P.emit("vector", lambda e: e.nop(), wait=[("tS", P.c("tS"))])
    P.emit("tensor", lambda e: e.nop(), wait=[("vS", P.c("vS"))])

    def emit_stream(e, name):
        for fn, waits, inc, inc_n in P.ops[name]:
            for s, v in waits:
                e.wait_ge(sem[s], v)
            ins = fn(e)
            if inc is not None:
                ins.then_inc(sem[inc], inc_n)

    @block.sync
    def _(e):
        emit_stream(e, "sync")

    @block.gpsimd
    def _(e):
        emit_stream(e, "gpsimd")

    @block.scalar
    def _(e):
        emit_stream(e, "scalar")

    @block.vector
    def _(e):
        emit_stream(e, "vector")

    @block.tensor
    def _(e):
        emit_stream(e, "tensor")

    ctx.close()
    return nc


def host_inputs(hidden, Wq, bq, Wo, M, z, W_base, b_base, W_router, lora_A, lora_B, T, n_cores):
    import ml_dtypes
    B, S, _ = hidden.shape
    x = np.ascontiguousarray(hidden.reshape(B * S, D))
    G = 4
    sel = np.concatenate([np.arange(h * G * HD, h * G * HD + HD) for h in range(NH)])
    aTdf = np.transpose(lora_A, (2, 0, 1)).reshape(D, 128)
    aTkm = aTdf.reshape(NK, 128, 128).transpose(1, 0, 2).reshape(128, D)
    shared = dict(
        wqT=np.ascontiguousarray(Wq[sel].T),
        bq8=np.ascontiguousarray(bq[sel].reshape(NH, HD).T),
        woT=np.ascontiguousarray(Wo.T),
        wbT=np.ascontiguousarray(W_base.T),
        bb32=np.ascontiguousarray(b_base.reshape(NK, 128).T),
        wrT=np.ascontiguousarray(W_router.T),
        aT=np.ascontiguousarray(aTkm),
        bT=np.ascontiguousarray(np.transpose(lora_B, (0, 2, 1)).reshape(128, D)),
        m3=np.ascontiguousarray(M),
        z8=np.ascontiguousarray(z.T),
        onesb=np.ones((128, 128), dtype=ml_dtypes.bfloat16),
        ident=np.eye(128, dtype=np.float32),
    )
    in_maps = []
    for c in range(n_cores):
        m = dict(shared)
        m["xT"] = np.ascontiguousarray(x[c * T:(c + 1) * T].T)
        in_maps.append(m)
    return in_maps


def kernel(L0_attn_hidden_states, L0_attn_Wq, L0_attn_bq, L0_attn_Wk, L0_attn_bk,
           L0_attn_Wv, L0_attn_bv, L0_attn_Wo, L0_attn_M, L0_attn_z,
           L0_ffn_W_base, L0_ffn_b_base, L0_ffn_W_router, L0_ffn_lora_A, L0_ffn_lora_B):
    B, S, _ = L0_attn_hidden_states.shape
    T = B * S // NCORE
    in_maps = host_inputs(L0_attn_hidden_states, L0_attn_Wq, L0_attn_bq, L0_attn_Wo,
                          L0_attn_M, L0_attn_z, L0_ffn_W_base, L0_ffn_b_base,
                          L0_ffn_W_router, L0_ffn_lora_A, L0_ffn_lora_B, T, NCORE)
    nc = build(T, 512)
    res = run_bass_kernel_spmd(nc, in_maps, core_ids=list(range(NCORE)))
    out = np.empty((B * S, D), dtype=np.float32)
    for c in range(NCORE):
        out[c * T:(c + 1) * T] = np.asarray(res.results[c]["outT"]).T
    return out.reshape(B, S, D)





# version 36
# speedup vs baseline: 1.1495x; 1.1495x over previous
import numpy as np
import concourse.bass as bass
import concourse.mybir as mybir
from concourse.bass_utils import run_bass_kernel_spmd
from contextlib import ExitStack


F32 = mybir.dt.float32
BF16 = mybir.dt.bfloat16
AF = mybir.ActivationFunctionType
ALU = mybir.AluOpType
AX = mybir.AxisListType

D = 4096
NK = 32
NH = 8
HD = 128
NCORE = 8
EPS_LN = 1e-5
EPS_DEN = 1e-6

ENGS = ("sync", "gpsimd", "scalar", "vector", "tensor")
SELF_SEM = {"vector": "vS", "scalar": "sS", "gpsimd": "gC"}
STREAMS = ("Wq", "Wo", "Wb", "At", "Bt")
SEMS = (("qC", "qY0", "qY1", "qY2", "qY3", "qX0", "qX1", "qX2", "qX3", "qSt0", "qSt1", "qSt2", "qSt3", "qF0", "qF1", "qF2", "qF3", "gS", "gC", "tS", "vS", "sS")
        + tuple(f"{s}{i}" for s in STREAMS for i in range(3))
        + tuple(f"V{s}{i}" for s in STREAMS for i in range(3)))


class Plan:
    def __init__(self):
        self.ops = {e: [] for e in ENGS}
        self.cnt = {}
        self.lastw = {e: {} for e in ENGS}

    def c(self, sem):
        return self.cnt.get(sem, 0)

    def emit(self, eng, fn, wait=(), inc=None, inc_n=1):
        wait = list(wait)
        own = SELF_SEM.get(eng)
        if own is not None:
            wait.append((own, self.c(own)))
            if inc is None:
                inc = own
        w = []
        for s, v in wait:
            if v <= 0 or self.lastw[eng].get(s, -1) >= v:
                continue
            self.lastw[eng][s] = v
            w.append((s, v))
        if inc is not None:
            self.cnt[inc] = self.c(inc) + inc_n
        self.ops[eng].append((fn, w, inc, inc_n))
        return self.c(inc) if inc is not None else None


def build(T, TT):
    nT = T // TT
    assert T % TT == 0 and 128 <= TT <= 512 and TT % 128 == 0
    nc = bass.Bass("TRN2", target_bir_lowering=False)

    xT = nc.dram_tensor("xT", [D, T], F32, kind="ExternalInput").ap()
    wqT = nc.dram_tensor("wqT", [D, NH * HD], F32, kind="ExternalInput").ap()
    bq8 = nc.dram_tensor("bq8", [128, NH], F32, kind="ExternalInput").ap()
    woT = nc.dram_tensor("woT", [D, D], F32, kind="ExternalInput").ap()
    wbT = nc.dram_tensor("wbT", [D, D], F32, kind="ExternalInput").ap()
    bb32 = nc.dram_tensor("bb32", [128, NK], F32, kind="ExternalInput").ap()
    wrT = nc.dram_tensor("wrT", [D, 8], F32, kind="ExternalInput").ap()
    aT = nc.dram_tensor("aT", [128, D], F32, kind="ExternalInput").ap()
    bT = nc.dram_tensor("bT", [128, D], F32, kind="ExternalInput").ap()
    m3 = nc.dram_tensor("m3", [NH, 128, 128], F32, kind="ExternalInput").ap()
    z8 = nc.dram_tensor("z8", [128, NH], F32, kind="ExternalInput").ap()
    onesb = nc.dram_tensor("onesb", [128, 128], BF16, kind="ExternalInput").ap()
    ident = nc.dram_tensor("ident", [128, 128], F32, kind="ExternalInput").ap()
    outT = nc.dram_tensor("outT", [D, T], F32, kind="ExternalOutput").ap()
    sWq = nc.dram_tensor("sWq", [NK, 128, 2048], BF16, kind="Internal").ap()
    sWo = nc.dram_tensor("sWo", [4 * NH, 128, 2048], BF16, kind="Internal").ap()
    sWb = nc.dram_tensor("sWb", [4 * NK, 128, 2048], BF16, kind="Internal").ap()
    sAT = nc.dram_tensor("sAT", [4, 128, 2048], BF16, kind="Internal").ap()
    sBT = nc.dram_tensor("sBT", [4, 128, 2048], BF16, kind="Internal").ap()

    ctx = ExitStack()
    E = ctx.enter_context

    xs = E(nc.sbuf_tensor("xs", [128, NK * TT], F32))
    xnh = E(nc.sbuf_tensor("xnh", [128, NK * TT], BF16))
    xnl = E(nc.sbuf_tensor("xnl", [128, NK * TT], BF16))
    mrow = E(nc.sbuf_tensor("mrow", [128, TT], F32))
    rrow = E(nc.sbuf_tensor("rrow", [128, TT], F32))
    t32a = E(nc.sbuf_tensor("t32aa", [128, TT], F32))
    t32b = E(nc.sbuf_tensor("t32b", [128, TT], F32))
    t32c = E(nc.sbuf_tensor("t32c", [128, TT], F32))
    h32 = E(nc.sbuf_tensor("h32", [128, TT], F32))
    hsh = E(nc.sbuf_tensor("hsh", [128, TT], BF16))
    hsl = E(nc.sbuf_tensor("hsl", [128, TT], BF16))
    xch = [E(nc.sbuf_tensor(f"xch{i}", [128, TT], BF16)) for i in range(2)]
    xcl = [E(nc.sbuf_tensor(f"xcl{i}", [128, TT], BF16)) for i in range(2)]
    xhib = xch
    sqb = xcl
    NSLOT = {"Wq": 2, "Wo": 2, "Wb": 3, "At": 1, "Bt": 1}
    wbuf = {s: [E(nc.sbuf_tensor(f"{s}hl{i}", [128, 2048], BF16)) for i in range(NSLOT[s])]
            for s in STREAMS}
    wst = [E(nc.sbuf_tensor(f"wst{i}", [128, 1024], F32)) for i in range(4)]
    m3s = E(nc.sbuf_tensor("m3s", [128, NH * 128], F32))
    z8s = E(nc.sbuf_tensor("z8s", [128, NH], F32))
    zrep = E(nc.sbuf_tensor("zrep", [128, 128], F32))
    wrh = E(nc.sbuf_tensor("wrh", [128, NK * 8], BF16))
    wrl = E(nc.sbuf_tensor("wrl", [128, NK * 8], BF16))
    bq8s = E(nc.sbuf_tensor("bq8s", [128, NH], F32))
    bb32s = E(nc.sbuf_tensor("bb32s", [128, NK], F32))
    onesbs = E(nc.sbuf_tensor("onesbs", [128, 128], BF16))
    idents = E(nc.sbuf_tensor("idents", [128, 128], F32))
    onesf = E(nc.sbuf_tensor("onesf", [128, 128], F32))
    epsc = E(nc.sbuf_tensor("epsc", [128, 1], F32))
    epsd = E(nc.sbuf_tensor("epsd", [128, 1], F32))
    lfm = E(nc.sbuf_tensor("lfm", [8, TT], F32))
    wtok = E(nc.sbuf_tensor("wtok", [128, 32], F32))
    wrep = E(nc.sbuf_tensor("wrep", [128, 128], F32))
    rcol = [E(nc.sbuf_tensor(f"rcol{i}", [128, 1], F32)) for i in range(4)]
    pb = [E(nc.psum_tensor(f"pb{i}", [128, 512], F32)) for i in range(8)]

    sem = {n: E(nc.semaphore(n)) for n in SEMS}
    block = E(nc.Block())

    P = Plan()
    sig32 = xnh[:].bitcast(F32)

    def dma(eng, q, out, in_, wait=(), accum=False):
        def f(e, out=out, in_=in_, accum=accum):
            if accum:
                return e.dma_start(out=out, in_=in_, accum_op=ALU.add)
            return e.dma_start(out=out, in_=in_)
        return P.emit(eng, f, wait=wait, inc=q, inc_n=16)

    dma("sync", "qC", bq8s[:], bq8)
    dma("sync", "qC", bb32s[:], bb32)
    dma("sync", "qC", onesbs[:], onesb)
    dma("sync", "qC", idents[:], ident)
    dma("sync", "qC", z8s[:], z8)
    wr32 = wst[0]
    for k in range(NK):
        dma("sync", "qC", wr32[:, k * 8:(k + 1) * 8], wrT[k * 128:(k + 1) * 128, :])
    for h in range(NH):
        dma("sync", "qC", m3s[:, h * 128:(h + 1) * 128], m3[h])
    CR = P.c("qC")
    P.emit("vector", lambda e: e.memset(onesf[:], 1.0))
    P.emit("vector", lambda e: e.memset(epsc[:], EPS_LN))
    epsd_v = P.emit("vector", lambda e: e.memset(epsd[:], EPS_DEN))
    wr_s = P.emit("scalar", lambda e: e.activation(out=wrh[:], in_=wr32[:, :NK * 8], func=AF.Copy, bias=0.0, scale=1.0),
                  wait=[("qC", CR)])
    wr_v = P.emit("vector", lambda e: e.tensor_sub(wrl[:], wr32[:, :NK * 8], wrh[:]),
                  wait=[("qC", CR), ("sS", wr_s)])
    st_free = [("vS", wr_v), ("vS", 0), ("vS", 0), ("vS", 0)]
    p0i = [0]

    class WS:

        def __init__(self, name, width, srcs_fn, scr):
            self.name, self.w = name, width
            self.ns = NSLOT[name]
            self.srcs_fn, self.scr = srcs_fn, scr
            self.free = [("tS", 0)] * self.ns
            self.milestone = None

        def get(self, idx, tile):
            i = idx % self.ns
            hb = wbuf[self.name][i][:, :self.w]
            lb = wbuf[self.name][i][:, self.w:2 * self.w]
            if tile == 0:
                srcs = self.srcs_fn(idx)
                if len(srcs) == 1:
                    sti = p0i[0] % 4
                    p0i[0] += 1
                    stg = wst[sti][:, :self.w]
                    stq = f"qSt{sti}"
                    dma("sync", stq, stg, srcs[0], wait=[st_free[sti]])
                    wq = [(stq, P.c(stq))]
                else:
                    tj = 2 * (p0i[0] % 2)
                    p0i[0] += 1
                    stg = wst[tj][:, :self.w]
                    stg1 = wst[tj + 1][:, :self.w]
                    dma("sync", f"qSt{tj}", stg, srcs[0], wait=[st_free[tj]])
                    pa = P.c(f"qSt{tj}")
                    dma("sync", f"qSt{tj + 1}", stg1, srcs[2], wait=[st_free[tj + 1]])
                    pbv = P.c(f"qSt{tj + 1}")
                    dma("gpsimd", f"qF{tj}", stg, srcs[1], wait=[(f"qSt{tj}", pa)], accum=True)
                    aa = P.c(f"qF{tj}")
                    dma("gpsimd", f"qF{tj + 1}", stg1, srcs[3], wait=[(f"qSt{tj + 1}", pbv)], accum=True)
                    ab = P.c(f"qF{tj + 1}")
                    vm_ = P.emit("vector", lambda e, o=stg, i1=stg1: e.tensor_add(o, o, i1),
                                 wait=[(f"qF{tj}", aa), (f"qF{tj + 1}", ab)])
                    st_free[tj + 1] = ("vS", vm_)
                    wq = [("vS", vm_)]
                vq = f"V{self.name}{i}"
                sh = P.emit("scalar", lambda e, o=hb, ii=stg: e.activation(
                    out=o, in_=ii, func=AF.Copy, bias=0.0, scale=1.0),
                    wait=wq + [self.free[i], (vq, P.c(vq))])
                sl = P.emit("vector", lambda e, o=lb, ii=stg, hh=hb: e.tensor_sub(o, ii, hh),
                            wait=wq + [("sS", sh), self.free[i], (vq, P.c(vq))])
                st_free[tj if len(srcs) > 1 else sti] = ("vS", sl)
                dma("scalar", vq, self.scr(idx), wbuf[self.name][i][:, :2 * self.w],
                    wait=[("sS", sh), ("vS", sl)])
                return hb, lb, [("sS", sh), ("vS", sl)]
            sq = f"{self.name}{i}"
            dma("sync", sq, wbuf[self.name][i][:, :2 * self.w], self.scr(idx),
                wait=[self.free[i], ("gS", self.milestone)])
            return hb, lb, [(sq, P.c(sq))]

        def done(self, idx, tS_val):
            self.free[idx % self.ns] = ("tS", tS_val)

        def finish_tile0(self):
            self.milestone = P.emit(
                "gpsimd", lambda e: e.nop(),
                wait=[(f"V{self.name}{i}", P.c(f"V{self.name}{i}")) for i in range(self.ns)],
                inc="gS")

    ws_q = WS("Wq", NH * HD, lambda k: [wqT[k * 128:(k + 1) * 128, :]],
              lambda k: sWq[k])
    ws_o = WS("Wo", 1024,
              lambda idx: [woT[(4 * (idx % NH) + g) * 128:(4 * (idx % NH) + g + 1) * 128,
                               (idx // NH) * 1024:(idx // NH + 1) * 1024] for g in range(4)],
              lambda idx: sWo[(idx // NH) * NH + idx % NH])
    ws_b = WS("Wb", 1024,
              lambda idx: [wbT[(idx % NK) * 128:(idx % NK + 1) * 128,
                               ((idx // NK) % 4) * 1024:((idx // NK) % 4 + 1) * 1024]],
              lambda idx: sWb[((idx // NK) % 4) * NK + idx % NK])
    ws_a = WS("At", 1024, lambda idx: [aT[:, idx * 1024:(idx + 1) * 1024]],
              lambda idx: sAT[idx])
    ws_t = WS("Bt", 1024, lambda idx: [bT[:, idx * 1024:(idx + 1) * 1024]],
              lambda idx: sBT[idx])

    st_done = [("qY0", 0), ("qY1", 0), ("qY2", 0), ("qY3", 0)]
    xhib_free = [("tS", 0)] * 2
    sqb_free = [("tS", 0)] * 2
    pb_free = [("vS", 0)] * 8
    ng = TT // 128

    def emit_xloads(tt, groups):
        out = []
        for g in groups:
            dma("scalar", f"qX{g}", xs[:, g * 8 * TT:(g + 1) * 8 * TT],
                bass.AP(xT.tensor, xT.offset + (g * 1024) * T + tt * TT,
                        [[T, 128], [128 * T, 8], [1, TT]]), wait=[st_done[g]])
            out.append((f"qX{g}", P.c(f"qX{g}")))
        return out

    next_loads = emit_xloads(0, range(4))
    for t in range(nT):
        tok0 = t * TT
        xload = [next_loads[k // 8] for k in range(NK)]
        next_loads_acc = []

        def ln_stats(gate):
            nonlocal xhib_free, sqb_free
            for k in range(NK):
                i = k % 2
                if k % 2 == 0:
                    vh = P.emit("vector", lambda e, k=k, i=i: e.tensor_copy(
                        xhib[i][:], xs[:, k * TT:(k + 1) * TT]),
                        wait=gate(k) + [xhib_free[i]])
                    hsem = "vS"
                    vq = P.emit("scalar", lambda e, i=i: e.activation(
                        out=sqb[i][:], in_=xhib[i][:], func=AF.Square, bias=0.0, scale=1.0),
                        wait=[sqb_free[i], ("vS", vh)])
                    qsem = "sS"
                else:
                    vh = P.emit("scalar", lambda e, k=k, i=i: e.activation(
                        out=xhib[i][:], in_=xs[:, k * TT:(k + 1) * TT], func=AF.Copy, bias=0.0, scale=1.0),
                        wait=gate(k) + [xhib_free[i]])
                    hsem = "sS"
                    vq = P.emit("vector", lambda e, i=i: e.tensor_mul(
                        sqb[i][:], xhib[i][:], xhib[i][:]),
                        wait=[sqb_free[i], ("sS", vh)])
                    qsem = "vS"
                tv = P.emit("tensor", lambda e, k=k, i=i: e.matmul(
                    pb[0][:, :TT], onesbs[:], xhib[i][:], start=(k == 0), stop=(k == NK - 1)),
                    wait=[(hsem, vh), ("qC", CR)] + ([pb_free[0]] if k == 0 else []), inc="tS")
                xhib_free[i] = ("tS", tv)
                tv2 = P.emit("tensor", lambda e, k=k, i=i: e.matmul(
                    pb[1][:, :TT], onesbs[:], sqb[i][:], start=(k == 0), stop=(k == NK - 1)),
                    wait=[(qsem, vq)] + ([pb_free[1]] if k == 0 else []), inc="tS")
                sqb_free[i] = ("tS", tv2)
            tdone = P.c("tS")
            v1 = P.emit("vector", lambda e: e.tensor_scalar(mrow[:], pb[0][:, :TT], 1.0 / D, None, ALU.mult),
                        wait=[("tS", tdone)])
            v2 = P.emit("vector", lambda e: e.tensor_mul(t32a[:], mrow[:], mrow[:]))
            v3 = P.emit("vector", lambda e: e.scalar_tensor_tensor(
                t32b[:], pb[1][:, :TT], 1.0 / D, t32a[:], ALU.mult, ALU.subtract))
            s1 = P.emit("scalar", lambda e: e.activation(out=t32c[:], in_=t32b[:], func=AF.Ln,
                                                         bias=epsc[:], scale=1.0), wait=[("vS", v3)])
            s2 = P.emit("scalar", lambda e: e.activation(out=rrow[:], in_=t32c[:], func=AF.Exp,
                                                         bias=0.0, scale=-0.5))
            pb_free[0] = ("vS", v1)
            pb_free[1] = ("vS", v3)
            return v1, ("sS", s2)

        m_r, r_r = ln_stats(lambda k: [xload[k]])

        prev_cast = 0
        for k in range(NK):
            i = k % 2
            va = P.emit("vector", lambda e, k=k: e.tensor_sub(t32a[:], xs[:, k * TT:(k + 1) * TT], mrow[:]),
                        wait=[("vS", m_r)])
            vsc = P.emit("vector", lambda e: e.tensor_mul(t32b[:], t32a[:], rrow[:]),
                         wait=[r_r, ("sS", prev_cast)])
            sh = P.emit("scalar", lambda e, i=i: e.activation(out=xch[i][:], in_=t32b[:], func=AF.Copy,
                                                              bias=0.0, scale=1.0),
                        wait=[("vS", vsc), xhib_free[i]])
            prev_cast = sh
            vl = P.emit("vector", lambda e, i=i: e.tensor_sub(xcl[i][:], t32b[:], xch[i][:]),
                        wait=[("sS", sh), sqb_free[i]])
            whb, wlb, wready = ws_q.get(k, t)
            tv = None
            for o in range(NH):
                P.emit("tensor", lambda e, w=whb, x=xch[i], o=o, st=(k == 0): e.matmul(
                    pb[o][:, :TT], w[:, o * 128:(o + 1) * 128], x[:], start=st, stop=False),
                    wait=[("sS", sh)] + wready + ([pb_free[o]] if k == 0 else []))
                P.emit("tensor", lambda e, w=whb, x=xcl[i], o=o: e.matmul(
                    pb[o][:, :TT], w[:, o * 128:(o + 1) * 128], x[:], start=False, stop=False),
                    wait=[("vS", vl)])
                tv = P.emit("tensor", lambda e, w=wlb, x=xch[i], o=o, lq=(k == NK - 1): e.matmul(
                    pb[o][:, :TT], w[:, o * 128:(o + 1) * 128], x[:], start=False, stop=lq),
                    inc="tS")
            xhib_free[i] = ("tS", tv)
            sqb_free[i] = ("tS", tv)
            ws_q.done(k, tv)
            if k == NK - 1:
                q_bank = list(range(P.c("tS") - NH + 1, P.c("tS") + 1))
        if t == 0:
            ws_q.finish_tile0()

        prev_vs = [0, 0]
        sig_v = []
        for o in range(NH):
            ea, eb = (t32c, t32b) if o % 2 == 0 else (mrow, rrow)
            se = P.emit("scalar", lambda e, o=o, ea=ea: e.activation(
                out=ea[:], in_=pb[o][:, :TT], func=AF.Exp, bias=bq8s[:, o:o + 1], scale=1.0),
                wait=[("tS", q_bank[o]), ("vS", prev_vs[o % 2]), ("qC", CR)])
            sr = P.emit("scalar", lambda e, o=o, eb=eb: e.activation(
                out=eb[:], in_=pb[o][:, :TT], func=AF.Relu, bias=bq8s[:, o:o + 1], scale=1.0))
            vs = P.emit("vector", lambda e, o=o, ea=ea, eb=eb: e.scalar_tensor_tensor(
                sig32[:, o * TT:(o + 1) * TT], ea[:], 1.0, eb[:], ALU.min, ALU.add),
                wait=[("sS", sr)])
            prev_vs[o % 2] = vs
            sig_v.append(vs)
            pb_free[o] = ("sS", sr)

        prev_td = 0
        prev_sh2 = 0
        prev_vl2 = 0
        retr_hv = []
        for h in range(NH):
            bn, bd = h % 4, 4 + h % 4
            vz = P.emit("vector", lambda e, h=h: e.tensor_copy(
                zrep[:], bass.AP(z8s[:].tensor, z8s[:].offset + h, [[NH, 128], [0, 128]])),
                wait=[("qC", CR), ("tS", prev_td)])
            tn = P.emit("tensor", lambda e, h=h, bn=bn: e.matmul(
                pb[bn][:, :TT], m3s[:, h * 128:(h + 1) * 128], sig32[:, h * TT:(h + 1) * TT],
                start=True, stop=True),
                wait=[("vS", sig_v[h]), pb_free[bn], ("qC", CR)], inc="tS")
            td = P.emit("tensor", lambda e, h=h, bd=bd: e.matmul(
                pb[bd][:, :TT], zrep[:], sig32[:, h * TT:(h + 1) * TT], start=True, stop=True),
                wait=[("vS", vz), pb_free[bd]], inc="tS")
            prev_td = td
            sd = P.emit("scalar", lambda e, bd=bd: e.activation(
                out=t32a[:], in_=pb[bd][:, :TT], func=AF.Ln, bias=epsd[:], scale=1.0),
                wait=[("tS", td), ("vS", epsd_v)])
            sr2 = P.emit("scalar", lambda e: e.activation(
                out=t32b[:], in_=t32a[:], func=AF.Exp, bias=0.0, scale=-1.0))
            vm2 = P.emit("vector", lambda e, bn=bn: e.tensor_mul(t32c[:], pb[bn][:, :TT], t32b[:]),
                         wait=[("tS", tn), ("sS", sr2), ("vS", prev_vl2)])
            sh2 = P.emit("scalar", lambda e, h=h: e.activation(
                out=xnl[:, h * TT:(h + 1) * TT], in_=t32c[:], func=AF.Copy, bias=0.0, scale=1.0),
                wait=[("vS", vm2)])
            prev_sh2 = sh2
            vl2 = P.emit("vector", lambda e, h=h: e.tensor_sub(
                xnl[:, (NH + h) * TT:(NH + h + 1) * TT], t32c[:], xnl[:, h * TT:(h + 1) * TT]),
                wait=[("sS", sh2)])
            prev_vl2 = vl2
            retr_hv.append([("sS", sh2), ("vS", vl2)])
            pb_free[bn] = ("vS", vm2)
            pb_free[bd] = ("sS", sd)

        P.emit("vector", lambda e: e.memset(mrow[:], 0.0), wait=[("vS", sig_v[7])])
        P.emit("vector", lambda e: e.memset(rrow[:], 0.0))
        st_prev = [retr_hv[7][1], retr_hv[7][1]]
        sacc_last = qacc_last = 0
        ev_done = []
        for og in range(4):
            tv_last = [None] * 8
            for h in range(NH):
                whb, wlb, wready = ws_o.get(og * NH + h, t)
                for oi in range(8):
                    P.emit("tensor", lambda e, w=whb, h=h, oi=oi: e.matmul(
                        pb[oi][:, :TT], w[:, oi * 128:(oi + 1) * 128], xnl[:, h * TT:(h + 1) * TT],
                        start=(h == 0), stop=False),
                        wait=wready + retr_hv[h] + ([pb_free[oi]] if h == 0 else []))
                    P.emit("tensor", lambda e, w=whb, h=h, oi=oi: e.matmul(
                        pb[oi][:, :TT], w[:, oi * 128:(oi + 1) * 128],
                        xnl[:, (NH + h) * TT:(NH + h + 1) * TT], start=False, stop=False))
                    r = P.emit("tensor", lambda e, w=wlb, h=h, oi=oi: e.matmul(
                        pb[oi][:, :TT], w[:, oi * 128:(oi + 1) * 128], xnl[:, h * TT:(h + 1) * TT],
                        start=False, stop=(h == NH - 1)),
                        inc="tS" if (oi == 7 or h == NH - 1) else None)
                    if h == NH - 1:
                        tv_last[oi] = r
                ws_o.done(og * NH + h, P.c("tS"))
            for oi in range(8):
                o = og * 8 + oi
                ve = P.emit("vector", lambda e, o=o, b=oi: e.tensor_add(
                    xs[:, o * TT:(o + 1) * TT], pb[b][:, :TT], xs[:, o * TT:(o + 1) * TT]),
                    wait=[("tS", tv_last[oi])])
                ev_done.append(ve)
                pb_free[oi] = ("vS", ve)
            for oi in range(8):
                k = og * 8 + oi
                i2 = k % 2
                tbuf = t32a if i2 == 0 else t32b
                sqo = P.emit("scalar", lambda e, k=k, tbuf=tbuf: e.activation(
                    out=tbuf[:], in_=xs[:, k * TT:(k + 1) * TT], func=AF.Square, bias=0.0, scale=1.0),
                    wait=[("vS", ev_done[k]), st_prev[i2]])
                sacc_last = P.emit("vector", lambda e, k=k: e.tensor_add(
                    mrow[:], mrow[:], xs[:, k * TT:(k + 1) * TT]))
                qacc_last = P.emit("vector", lambda e, tbuf=tbuf: e.tensor_add(
                    rrow[:], rrow[:], tbuf[:]), wait=[("sS", sqo)])
                st_prev[i2] = ("vS", qacc_last)
        if t == 0:
            ws_o.finish_tile0()

        tm0 = P.emit("tensor", lambda e: e.matmul(
            pb[0][:, :TT], onesf[:], mrow[:], start=True, stop=True),
            wait=[("vS", sacc_last), pb_free[0], ("qC", CR)], inc="tS")
        tm1 = P.emit("tensor", lambda e: e.matmul(
            pb[1][:, :TT], onesf[:], rrow[:], start=True, stop=True),
            wait=[("vS", qacc_last), pb_free[1]], inc="tS")
        f_v1 = P.emit("vector", lambda e: e.tensor_scalar(mrow[:], pb[0][:, :TT], 1.0 / D, None, ALU.mult),
                      wait=[("tS", tm1)])
        P.emit("vector", lambda e: e.tensor_mul(t32a[:], mrow[:], mrow[:]))
        f_v3 = P.emit("vector", lambda e: e.scalar_tensor_tensor(
            t32b[:], pb[1][:, :TT], 1.0 / D, t32a[:], ALU.mult, ALU.subtract))
        f_s1 = P.emit("scalar", lambda e: e.activation(out=t32c[:], in_=t32b[:], func=AF.Ln,
                                                       bias=epsc[:], scale=1.0), wait=[("vS", f_v3)])
        f_s2 = P.emit("scalar", lambda e: e.activation(out=rrow[:], in_=t32c[:], func=AF.Exp,
                                                       bias=0.0, scale=-0.5))
        pb_free[0] = ("vS", f_v1)
        pb_free[1] = ("vS", f_v3)
        m_r, r_r = f_v1, ("sS", f_s2)

        pre_a = ws_a.get(0, t)
        pre_t0 = ws_t.get(0, t)
        pre_b = [ws_b.get(k, t) for k in range(3)]

        bhb0, blb0, bready0 = pre_t0
        ahb = alb = None
        aready = []
        prev_gsh = 0
        xn_k = []
        for k in range(NK):
            va = P.emit("vector", lambda e, k=k: e.tensor_sub(t32a[:], xs[:, k * TT:(k + 1) * TT], mrow[:]),
                        wait=[r_r, ("sS", prev_gsh)])
            vb = P.emit("vector", lambda e: e.tensor_mul(t32b[:], t32a[:], rrow[:]))
            gsh = P.emit("scalar", lambda e, k=k: e.activation(
                out=xnh[:, k * TT:(k + 1) * TT], in_=t32b[:], func=AF.Copy, bias=0.0, scale=1.0),
                wait=[("vS", vb)])
            prev_gsh = gsh
            gvl = P.emit("vector", lambda e, k=k: e.tensor_sub(
                xnl[:, k * TT:(k + 1) * TT], t32b[:], xnh[:, k * TT:(k + 1) * TT]),
                wait=[("sS", gsh)])
            xn_k.append([("sS", gsh), ("vS", gvl)])
            if k % 8 == 0:
                ahb, alb, aready = pre_a if k == 0 else ws_a.get(k // 8, t)
            kk = k % 8
            P.emit("tensor", lambda e, k=k: e.matmul(
                pb[1][0:8, :TT], wrh[:, k * 8:(k + 1) * 8], xnh[:, k * TT:(k + 1) * TT],
                start=(k == 0), stop=False),
                wait=[("sS", wr_s), ("vS", wr_v)] + xn_k[k] + ([pb_free[1]] if k == 0 else []))
            P.emit("tensor", lambda e, k=k: e.matmul(
                pb[1][0:8, :TT], wrh[:, k * 8:(k + 1) * 8], xnl[:, k * TT:(k + 1) * TT],
                start=False, stop=False))
            P.emit("tensor", lambda e, k=k: e.matmul(
                pb[1][0:8, :TT], wrl[:, k * 8:(k + 1) * 8], xnh[:, k * TT:(k + 1) * TT],
                start=False, stop=(k == NK - 1)), inc="tS" if k == NK - 1 else None)
            P.emit("tensor", lambda e, w=ahb, k=k, kk=kk: e.matmul(
                pb[0][:, :TT], w[:, kk * 128:(kk + 1) * 128], xnh[:, k * TT:(k + 1) * TT],
                start=(k == 0), stop=False),
                wait=aready + ([pb_free[0]] if k == 0 else []))
            P.emit("tensor", lambda e, w=ahb, k=k, kk=kk: e.matmul(
                pb[0][:, :TT], w[:, kk * 128:(kk + 1) * 128], xnl[:, k * TT:(k + 1) * TT],
                start=False, stop=False))
            P.emit("tensor", lambda e, w=alb, k=k, kk=kk: e.matmul(
                pb[0][:, :TT], w[:, kk * 128:(kk + 1) * 128], xnh[:, k * TT:(k + 1) * TT],
                start=False, stop=(k == NK - 1)), inc="tS" if k % 8 == 7 else None)
            if k % 8 == 7:
                ws_a.done(k // 8, P.c("tS"))
            whb, wlb, wready = pre_b[k] if k < 3 else ws_b.get(0 * NK + k, t)
            for oi in range(6):
                b = 2 + oi
                P.emit("tensor", lambda e, w=whb, oi=oi, k=k, b=b: e.matmul(
                    pb[b][:, :TT], w[:, oi * 128:(oi + 1) * 128], xnh[:, k * TT:(k + 1) * TT],
                    start=(k == 0), stop=False),
                    wait=wready + ([pb_free[b]] if k == 0 else []))
                P.emit("tensor", lambda e, w=whb, oi=oi, k=k, b=b: e.matmul(
                    pb[b][:, :TT], w[:, oi * 128:(oi + 1) * 128], xnl[:, k * TT:(k + 1) * TT],
                    start=False, stop=False))
                P.emit("tensor", lambda e, w=wlb, oi=oi, k=k, b=b: e.matmul(
                    pb[b][:, :TT], w[:, oi * 128:(oi + 1) * 128], xnh[:, k * TT:(k + 1) * TT],
                    start=False, stop=False),
                    inc="tS" if oi == 5 else None)
            ws_b.done(0 * NK + k, P.c("tS"))
        la_done = P.c("tS")
        vlf = P.emit("vector", lambda e: e.tensor_copy(lfm[:], pb[1][0:8, :TT]), wait=[("tS", la_done)])
        vh32 = P.emit("vector", lambda e: e.tensor_copy(h32[:], pb[0][:, :TT]))
        pb_free[0] = ("vS", vh32)
        pb_free[1] = ("vS", vlf)
        ttr = []
        lt_read = {0: [("vS", vh32)], 1: [("vS", vlf)]}
        for g in range(ng):
            b = g % 2
            tg = P.emit("tensor", lambda e, g=g, b=b: e.transpose(
                pb[b][:, 0:8], lfm[0:8, g * 128:(g + 1) * 128], idents[0:8, 0:8]),
                wait=lt_read[b] + [("qC", CR)], inc="tS")
            ttr.append(tg)
            if g % 2 == 1 or g == ng - 1:
                for gg in range(g - (g % 2), g + 1):
                    bb_ = gg % 2
                    lt = pb[bb_][:, 0:8]
                    vm1 = P.emit("vector", lambda e, lt=lt: e.reduce_max(rcol[0][:], lt, axis=AX.X),
                                 wait=[("tS", ttr[gg])])
                    vn1 = P.emit("vector", lambda e: e.tensor_scalar(rcol[1][:], rcol[0][:], -1.0, None, ALU.mult))
                    veq = P.emit("vector", lambda e, lt=lt: e.tensor_scalar(
                        t32a[:, :8], lt, rcol[0][:], None, ALU.is_equal))
                    vl2 = P.emit("vector", lambda e, lt=lt: e.scalar_tensor_tensor(
                        t32b[:, :8], t32a[:, :8], -1e30, lt, ALU.mult, ALU.add))
                    vm2 = P.emit("vector", lambda e: e.reduce_max(rcol[2][:], t32b[:, :8], axis=AX.X))
                    vke = P.emit("vector", lambda e, lt=lt: e.tensor_scalar(
                        t32c[:, :8], lt, rcol[2][:], None, ALU.is_ge))
                    sex = P.emit("scalar", lambda e, lt=lt: e.activation(
                        out=t32b[:, :8], in_=lt, func=AF.Exp, bias=rcol[1][:], scale=1.0),
                        wait=[("vS", vke)])
                    vek = P.emit("vector", lambda e: e.tensor_mul(t32b[:, :8], t32b[:, :8], t32c[:, :8]),
                                 wait=[("sS", sex)])
                    vsm = P.emit("vector", lambda e: e.reduce_sum(rcol[3][:], t32b[:, :8], axis=AX.X))
                    vrc = P.emit("vector", lambda e: e.reciprocal(rcol[3][:], rcol[3][:]))
                    wt_ready = P.emit("vector", lambda e, gg=gg: e.tensor_scalar(
                        wtok[:, gg * 8:(gg + 1) * 8], t32b[:, :8], rcol[3][:], 2.0, ALU.mult, ALU.mult))
                    lt_read[bb_] = [("sS", sex), ("vS", vl2)]
        if t == 0:
            ws_a.finish_tile0()
            ws_b.milestone = P.emit(
                "gpsimd", lambda e: e.nop(),
                wait=[(f"VWb{i}", P.c(f"VWb{i}")) for i in range(ws_b.ns)], inc="gS")
        for k in range(NK):
            whb, wlb, wready = ws_b.get(4 * NK + k, max(t, 1))
            for j in range(2):
                oi = 6 + j
                P.emit("tensor", lambda e, w=whb, oi=oi, k=k, b=j: e.matmul(
                    pb[b][:, :TT], w[:, oi * 128:(oi + 1) * 128], xnh[:, k * TT:(k + 1) * TT],
                    start=(k == 0), stop=False),
                    wait=wready + (lt_read[j] if k == 0 else []))
                P.emit("tensor", lambda e, w=whb, oi=oi, k=k, b=j: e.matmul(
                    pb[b][:, :TT], w[:, oi * 128:(oi + 1) * 128], xnl[:, k * TT:(k + 1) * TT],
                    start=False, stop=False))
                P.emit("tensor", lambda e, w=wlb, oi=oi, k=k, b=j: e.matmul(
                    pb[b][:, :TT], w[:, oi * 128:(oi + 1) * 128], xnh[:, k * TT:(k + 1) * TT],
                    start=False, stop=(k == NK - 1)),
                    inc="tS" if j == 1 else None)
            ws_b.done(4 * NK + k, P.c("tS"))
        res_done = P.c("tS")
        ev6 = P.emit("vector", lambda e: e.tensor_copy(mrow[:], pb[0][:, :TT]), wait=[("tS", res_done)])
        ev7 = P.emit("vector", lambda e: e.tensor_copy(rrow[:], pb[1][:, :TT]))
        wfm_read = {0: ("vS", ev6), 1: ("vS", ev7)}
        vhs_l = []
        for g in range(ng):
            b = g % 2
            vrep = P.emit("vector", lambda e, g=g: e.tensor_copy(
                wrep[:].rearrange("p (a b) -> p a b", a=8),
                bass.AP(wtok[:].tensor, wtok[:].offset + g * 8, [[32, 128], [1, 8], [0, 16]])),
                wait=[("vS", wt_ready)])
            twf = P.emit("tensor", lambda e, g=g, b=b: e.transpose(
                pb[b][:, 0:128], wrep[:], idents[:]),
                wait=[("vS", vrep), wfm_read[b]], inc="tS")
            vhs = P.emit("vector", lambda e, g=g, b=b: e.tensor_mul(
                t32a[:, g * 128:(g + 1) * 128], h32[:, g * 128:(g + 1) * 128], pb[b][:, 0:128]),
                wait=[("tS", twf)])
            wfm_read[b] = ("vS", vhs)
            vhs_l.append(vhs)
        shh = P.emit("scalar", lambda e: e.activation(out=hsh[:], in_=t32a[:], func=AF.Copy,
                                                      bias=0.0, scale=1.0),
                     wait=[("vS", P.c("vS"))])
        vhl = P.emit("vector", lambda e: e.tensor_sub(hsl[:], t32a[:], hsh[:]), wait=[("sS", shh)])
        hs_s, hs_v = shh, vhl
        pre_nb = ws_b.get(1 * NK, t)
        for oi in range(8):
            o = oi
            b = 2 + oi if oi < 6 else oi - 6
            st_ = (oi >= 6)
            P.emit("tensor", lambda e, w=bhb0, b=b, oi=oi, st_=st_: e.matmul(
                pb[b][:, :TT], w[:, oi * 128:(oi + 1) * 128], hsh[:], start=st_, stop=False),
                wait=bready0 + [("sS", hs_s), ("vS", hs_v)] + ([wfm_read[b]] if st_ else []))
            P.emit("tensor", lambda e, w=bhb0, b=b, oi=oi: e.matmul(
                pb[b][:, :TT], w[:, oi * 128:(oi + 1) * 128], hsl[:], start=False, stop=False))
            tb = P.emit("tensor", lambda e, w=blb0, b=b, oi=oi: e.matmul(
                pb[b][:, :TT], w[:, oi * 128:(oi + 1) * 128], hsh[:], start=False, stop=True),
                inc="tS")
            if oi < 6:
                vf = P.emit("vector", lambda e, o=o, b=b: e.scalar_tensor_tensor(
                    xs[:, o * TT:(o + 1) * TT], pb[b][:, :TT], bb32s[:, o:o + 1],
                    xs[:, o * TT:(o + 1) * TT], ALU.add, ALU.add),
                    wait=[("tS", tb)])
            else:
                base_sb = mrow if oi == 6 else rrow
                tmp = t32c if oi == 6 else t32b
                vfa = P.emit("vector", lambda e, o=o, b=b, base_sb=base_sb, tmp=tmp: e.scalar_tensor_tensor(
                    tmp[:], pb[b][:, :TT], bb32s[:, o:o + 1], base_sb[:], ALU.add, ALU.add),
                    wait=[("tS", tb)])
                vf = P.emit("vector", lambda e, o=o, tmp=tmp: e.tensor_add(
                    xs[:, o * TT:(o + 1) * TT], tmp[:], xs[:, o * TT:(o + 1) * TT]))
            pb_free[b] = ("vS", vf)
        dma("scalar", "qY0",
            bass.AP(outT.tensor, outT.offset + tok0, [[T, 128], [128 * T, 8], [1, TT]]),
            xs[:, :8 * TT], wait=[("vS", vf)])
        ws_t.done(0, P.c("tS"))
        if t < nT - 1:
            st_done[0] = ("qY0", P.c("qY0"))
            next_loads_acc += emit_xloads(t + 1, [0])
        for og in range(1, 4):
            bhb, blb, bready = ws_t.get(og, t)
            for k in range(NK):
                whb, wlb, wready = pre_nb if k == 0 else ws_b.get(og * NK + k, t)
                for oi in range(8):
                    P.emit("tensor", lambda e, w=whb, oi=oi, k=k: e.matmul(
                        pb[oi][:, :TT], w[:, oi * 128:(oi + 1) * 128], xnh[:, k * TT:(k + 1) * TT],
                        start=(k == 0), stop=False),
                        wait=wready + ([pb_free[oi]] if k == 0 else []))
                    P.emit("tensor", lambda e, w=whb, oi=oi, k=k: e.matmul(
                        pb[oi][:, :TT], w[:, oi * 128:(oi + 1) * 128], xnl[:, k * TT:(k + 1) * TT],
                        start=False, stop=False))
                    P.emit("tensor", lambda e, w=wlb, oi=oi, k=k: e.matmul(
                        pb[oi][:, :TT], w[:, oi * 128:(oi + 1) * 128], xnh[:, k * TT:(k + 1) * TT],
                        start=False, stop=False),
                        inc="tS" if oi == 7 else None)
                ws_b.done(og * NK + k, P.c("tS"))
                if k == NK - 1 and og < 3:
                    pre_nb = ws_b.get((og + 1) * NK, t)
            for oi in range(8):
                o = og * 8 + oi
                P.emit("tensor", lambda e, w=bhb, b=oi, oi=oi: e.matmul(
                    pb[b][:, :TT], w[:, oi * 128:(oi + 1) * 128], hsh[:], start=False, stop=False),
                    wait=bready + [("sS", hs_s), ("vS", hs_v)])
                P.emit("tensor", lambda e, w=bhb, b=oi, oi=oi: e.matmul(
                    pb[b][:, :TT], w[:, oi * 128:(oi + 1) * 128], hsl[:], start=False, stop=False))
                tb = P.emit("tensor", lambda e, w=blb, b=oi, oi=oi: e.matmul(
                    pb[b][:, :TT], w[:, oi * 128:(oi + 1) * 128], hsh[:], start=False, stop=True),
                    inc="tS")
                vf = P.emit("vector", lambda e, o=o, b=oi: e.scalar_tensor_tensor(
                    xs[:, o * TT:(o + 1) * TT], pb[b][:, :TT], bb32s[:, o:o + 1],
                    xs[:, o * TT:(o + 1) * TT], ALU.add, ALU.add),
                    wait=[("tS", tb)])
                pb_free[oi] = ("vS", vf)
            dma("scalar", f"qY{og}",
                bass.AP(outT.tensor, outT.offset + (og * 1024) * T + tok0,
                        [[T, 128], [128 * T, 8], [1, TT]]),
            xs[:, og * 8 * TT:(og + 1) * 8 * TT], wait=[("vS", vf)])
            ws_t.done(og, P.c("tS"))
            if t < nT - 1:
                st_done[og] = (f"qY{og}", P.c(f"qY{og}"))
                next_loads_acc += emit_xloads(t + 1, [og])
        if t == 0:
            ws_b.finish_tile0()
            ws_t.finish_tile0()
        next_loads = next_loads_acc

    P.emit("scalar", lambda e: e.nop(), wait=[(f"qY{i}", P.c(f"qY{i}")) for i in range(4)] + [(f"qX{i}", P.c(f"qX{i}")) for i in range(4)], inc="sS")
    P.emit("sync", lambda e: e.nop(),
           wait=[(s, P.c(s)) for s in
                 [f"{s}{i}" for s in STREAMS for i in range(2)] + ["qSt0", "qSt1", "qSt2", "qSt3", "qC"]])
    P.emit("gpsimd", lambda e: e.nop(),
           wait=[(s, P.c(s)) for s in
                 [f"V{s}{i}" for s in STREAMS for i in range(2)] + ["qF0", "qF1", "gS"]])
    P.emit("vector", lambda e: e.nop(), wait=[("tS", P.c("tS"))])
    P.emit("tensor", lambda e: e.nop(), wait=[("vS", P.c("vS"))])

    def emit_stream(e, name):
        for fn, waits, inc, inc_n in P.ops[name]:
            for s, v in waits:
                e.wait_ge(sem[s], v)
            ins = fn(e)
            if inc is not None:
                ins.then_inc(sem[inc], inc_n)

    @block.sync
    def _(e):
        emit_stream(e, "sync")

    @block.gpsimd
    def _(e):
        emit_stream(e, "gpsimd")

    @block.scalar
    def _(e):
        emit_stream(e, "scalar")

    @block.vector
    def _(e):
        emit_stream(e, "vector")

    @block.tensor
    def _(e):
        emit_stream(e, "tensor")

    ctx.close()
    return nc


def host_inputs(hidden, Wq, bq, Wo, M, z, W_base, b_base, W_router, lora_A, lora_B, T, n_cores):
    import ml_dtypes
    B, S, _ = hidden.shape
    x = np.ascontiguousarray(hidden.reshape(B * S, D))
    G = 4
    sel = np.concatenate([np.arange(h * G * HD, h * G * HD + HD) for h in range(NH)])
    aTdf = np.transpose(lora_A, (2, 0, 1)).reshape(D, 128)
    aTkm = aTdf.reshape(NK, 128, 128).transpose(1, 0, 2).reshape(128, D)
    shared = dict(
        wqT=np.ascontiguousarray(Wq[sel].T),
        bq8=np.ascontiguousarray(bq[sel].reshape(NH, HD).T),
        woT=np.ascontiguousarray(Wo.T),
        wbT=np.ascontiguousarray(W_base.T),
        bb32=np.ascontiguousarray(b_base.reshape(NK, 128).T),
        wrT=np.ascontiguousarray(W_router.T),
        aT=np.ascontiguousarray(aTkm),
        bT=np.ascontiguousarray(np.transpose(lora_B, (0, 2, 1)).reshape(128, D)),
        m3=np.ascontiguousarray(M),
        z8=np.ascontiguousarray(z.T),
        onesb=np.ones((128, 128), dtype=ml_dtypes.bfloat16),
        ident=np.eye(128, dtype=np.float32),
    )
    in_maps = []
    for c in range(n_cores):
        m = dict(shared)
        m["xT"] = np.ascontiguousarray(x[c * T:(c + 1) * T].T)
        in_maps.append(m)
    return in_maps


def kernel(L0_attn_hidden_states, L0_attn_Wq, L0_attn_bq, L0_attn_Wk, L0_attn_bk,
           L0_attn_Wv, L0_attn_bv, L0_attn_Wo, L0_attn_M, L0_attn_z,
           L0_ffn_W_base, L0_ffn_b_base, L0_ffn_W_router, L0_ffn_lora_A, L0_ffn_lora_B):
    B, S, _ = L0_attn_hidden_states.shape
    T = B * S // NCORE
    in_maps = host_inputs(L0_attn_hidden_states, L0_attn_Wq, L0_attn_bq, L0_attn_Wo,
                          L0_attn_M, L0_attn_z, L0_ffn_W_base, L0_ffn_b_base,
                          L0_ffn_W_router, L0_ffn_lora_A, L0_ffn_lora_B, T, NCORE)
    nc = build(T, 512)
    res = run_bass_kernel_spmd(nc, in_maps, core_ids=list(range(NCORE)))
    out = np.empty((B * S, D), dtype=np.float32)
    for c in range(NCORE):
        out[c * T:(c + 1) * T] = np.asarray(res.results[c]["outT"]).T
    return out.reshape(B, S, D)



